# version 02 (final)
# baseline (speedup 1.0000x reference)
import numpy as np
import concourse.bass as bass
import concourse.mybir as mybir
from concourse.bass_utils import run_bass_kernel_spmd
from contextlib import ExitStack

F32 = mybir.dt.float32
I32 = mybir.dt.int32
AF = mybir.ActivationFunctionType
ALU = mybir.AluOpType
AXX = mybir.AxisListType.X

NCORE = 8
TPC = 2048
D = 1024
DI = 2816
NCH = 8
NJ = 22
TTW = 512
NTT = TPC // TTW
EPS = 1e-5


def _flat(waits):
    out = []
    for x in waits:
        if x is None:
            continue
        if isinstance(x, list):
            out.extend(t for t in x if t)
        elif x[1]:
            out.append(x)
    return [t for t in out if t[1] and t[1] > 0]


class P:
    ENG = ("v", "s", "t", "y", "g")
    SEMS = ("v", "s", "t", "c",
            "y0", "y1", "y2", "y3", "g0", "g1", "g2", "g3")
    NROT = 4

    def __init__(self):
        self.stream = {e: [] for e in self.ENG}
        self.cnt = {s: 0 for s in self.SEMS}
        self.rot = {"y": 0, "g": 0}

    def op(self, e, fn, waits=(), sem=None):
        sem = sem or e
        w = _flat(waits)
        self.cnt[sem] += 1
        self.stream[e].append((w, fn, sem, 1))
        return self.cnt[sem]

    def dma(self, q, fn, waits=()):
        k = self.rot[q]
        self.rot[q] = (k + 1) % self.NROT
        sem = f"{q}{k}"
        w = _flat(waits)
        if self.cnt[sem] > 0:
            w.append((sem, self.cnt[sem]))
        self.cnt[sem] += 16
        self.stream[q].append((w, fn, sem, 16))
        return (sem, self.cnt[sem])

    def qfence(self, q):
        return [(f"{q}{k}", self.cnt[f"{q}{k}"]) for k in range(self.NROT)
                if self.cnt[f"{q}{k}"] > 0]

    def bare_wait(self, e, waits):
        self.stream[e].append((_flat(waits), None, None, 0))


def build(stage="full", dbg=()):
    nc = bass.Bass("TRN2", target_bir_lowering=False, debug=False,
                   detect_race_conditions=False)
    dr = lambda n, s, k: nc.dram_tensor(n, s, F32, kind=k).ap()
    x_in = dr("x", [TPC, D], "ExternalInput")
    wgT = dr("wgT", [D, DI], "ExternalInput")
    wuT = dr("wuT", [D, DI], "ExternalInput")
    wdT = dr("wdT", [DI, D], "ExternalInput")
    wkT = dr("wkT", [D, D], "ExternalInput")
    wvT = dr("wvT", [D, D], "ExternalInput")
    wrT = dr("wrT", [D, D], "ExternalInput")
    woT = dr("woT", [D, D], "ExternalInput")
    tmix = dr("tmix", [6, D], "ExternalInput")
    mprev_d = dr("mprev", [128, 8], "ExternalInput")
    msel_d = dr("msel", [128, 8], "ExternalInput")
    ident_d = dr("ident", [128, 128], "ExternalInput")
    out_d = dr("out", [TPC, D], "ExternalOutput")
    dbg_d = dr("dbg", [128, 8192], "ExternalOutput") if dbg else None
    sc_gu = dr("sc_gu", [NJ, 2, NCH, 128, 128], "Internal")
    sc_wd = dr("sc_wd", [NJ, NCH, 128, 128], "Internal")
    sc_park = dr("sc_park", [NCH, 128, TPC], "Internal")
    sc_k = dr("sc_k", [NCH, 128, TPC], "Internal")
    sc_e = dr("sc_e", [NCH, 3, 128, TPC], "Internal")
    cc_h_in = dr("cc_h_in", [128, 8], "Internal")
    cc_h_out = dr("cc_h_out", [NCORE, 128, 8], "Internal")
    cc_e_in = dr("cc_e_in", [128, 24], "Internal")
    cc_e_out = dr("cc_e_out", [NCORE, 128, 24], "Internal")

    ctx = ExitStack()
    with ctx:
        e_ = ctx.enter_context
        sb = lambda n, c, dt=F32: e_(nc.sbuf_tensor(n, [128, c], dt))
        R = sb("R", NCH * TPC)
        L = sb("L", 8192)
        XQ = sb("XQ", NCH * TTW)
        HQ = sb("HQ", NJ * TTW)
        GUS = sb("GUS", 4096)
        WDS = sb("WDS", 2048)
        SQ = sb("SQ", 1024)
        STF = sb("STF", 1024)
        STW = SQ
        CN = sb("CN", 896)
        psA = e_(nc.psum_tensor("psA", [128, 512], F32))
        psB = e_(nc.psum_tensor("psB", [128, 512], F32))
        psC = e_(nc.psum_tensor("psC", [128, 512], F32))
        psD = e_(nc.psum_tensor("psD", [128, 512], F32))
        psE = e_(nc.psum_tensor("psE", [128, 1024], F32))
        psF = e_(nc.psum_tensor("psF", [128, 1024], F32))
        sems = {k: e_(nc.semaphore("sem_" + k)) for k in P.SEMS}
        block = e_(nc.Block())
        p = P()

        b1, b2 = L[:, 0:2048], L[:, 2048:4096]
        b3, b4 = L[:, 4096:6144], L[:, 6144:8192]
        qb = {"yn": L[:, 4096:4608], "A": L[:, 4608:5120],
              "B": L[:, 5120:5632], "xq": L[:, 5632:6144]}
        wib = L[:, 6144:6656]
        wrp = L[:, 6656:7168]
        hb = L[:, 7168:7680]
        hb2 = L[:, 7680:8192]
        wrp2 = L[:, 2048:2560]
        qsm = L[:, 2560:2816]
        SGN = L[:, 2816:3328]
        tmc = lambda ch: CN[:, ch * 6:(ch + 1) * 6]
        dvc = lambda ch: CN[:, 48 + ch * 6:48 + ch * 6 + 6]
        MPREV = CN[:, 96:104]
        MSEL = CN[:, 104:112]
        IDN = CN[:, 112:240]
        ENDS = CN[:, 240:264]
        EG = CN[:, 264:456]
        AINW = CN[:, 456:520]
        BINW = CN[:, 520:584]
        AIN = CN[:, 584:592]
        BIN = CN[:, 592:600]
        HG = CN[:, 600:664]
        HAL = CN[:, 664:672]
        P1E = CN[:, 680:688]
        AFULL = CN[:, 688:696]
        SQS = CN[:, 704:720]
        ONESC = CN[:, 720:721]
        EPSC = CN[:, 721:722]
        W8 = CN[:, 724:732]
        ONESR = CN[0:1, 736:864]
        JUNK = CN[:, 864:865]

        def mixslot(ch):
            if ch < 5:
                return HQ[:, ch * 2048:(ch + 1) * 2048]
            if ch < 7:
                return XQ[:, (ch - 5) * 2048:(ch - 4) * 2048]
            return WDS[:, 0:2048]

        def dma(q, o, i, waits=()):
            return p.dma(q, lambda e, o=o, i=i: e.dma_start(out=o, in_=i),
                         waits=waits)

        def vop(fn, waits=()):
            return p.op("v", fn, waits)

        def sop(fn, waits=()):
            return p.op("s", fn, waits)

        def top(fn, waits=()):
            return p.op("t", fn, waits)

        def gop(fn, waits=()):
            return p.op("g", fn, waits, sem="c")

        dbg_off = [0]

        def dump(tag, buf, width):
            if tag in dbg:
                off = dbg_off[0]
                dbg_off[0] += width
                dma("g", dbg_d[:, off:off + width], buf,
                    waits=[("v", p.cnt["v"]), ("s", p.cnt["s"])])

        def vjunk():
            return vop(lambda e: e.tensor_copy(JUNK, JUNK))

        def vspecial(fn, waits=()):
            c0 = p.cnt["v"]
            c = vop(fn, waits=list(waits) + [("v", c0)])
            p.bare_wait("v", [("v", c)])
            return c

        def vtrans(out_ap, in_ap, waits=()):
            return vspecial(lambda e, o=out_ap, i=in_ap: e.transpose(o, i),
                            waits)

        def vrecip(out_ap, in_ap, waits=()):
            return vspecial(lambda e, o=out_ap, i=in_ap: e.reciprocal(o, i),
                            waits)

        def vscan(out_ap, d0, d1, init, waits=()):
            return vspecial(
                lambda e, o=out_ap, a=d0, b=d1, i=init: e.tensor_tensor_scan(
                    o, a, b, i, op0=ALU.mult, op1=ALU.add), waits)

        def quant(src, W, waits=()):
            NB = W // 16
            r3 = lambda t: t[:, :W].rearrange("p (b k) -> p b k", k=16)
            bm, scl = qsm[:, 0:NB], qsm[:, 64:64 + NB]
            rscl = qsm[:, 192:192 + NB]
            bc = lambda t: t.rearrange("p (b one) -> p b one", one=1) \
                            .broadcast_to([128, NB, 16])
            sgc = sop(lambda e: e.sign(SGN[:, :W], src[:, :W]),
                      waits=[("v", p.cnt["v"])])
            vspecial(lambda e: e.reduce_max(
                bm.rearrange("p (b one) -> p b one", one=1), r3(src),
                axis=AXX, apply_absolute_value=True), waits)
            vop(lambda e: e.tensor_scalar(scl, bm, 1.0 / 6.0, 1e-12,
                                          ALU.mult, ALU.max))
            vrecip(rscl, scl)
            yn, A, B, xq = qb["yn"], qb["A"], qb["B"], qb["xq"]
            vop(lambda e: e.tensor_tensor(r3(yn), r3(src), bc(rscl), ALU.mult))
            vop(lambda e: e.tensor_tensor(A[:, :W], yn[:, :W], SGN[:, :W],
                                          ALU.mult), waits=[("s", sgc)])
            vop(lambda e: e.tensor_scalar(B[:, :W], A[:, :W], 0.25, None,
                                          ALU.is_gt))
            for mid in (0.75, 1.25, 1.75, 2.5, 3.5, 5.0):
                vop(lambda e, mid=mid: e.scalar_tensor_tensor(
                    B[:, :W], A[:, :W], mid, B[:, :W], ALU.is_gt, ALU.add))
            vop(lambda e: e.tensor_scalar(yn[:, :W], B[:, :W], 6.0, None,
                                          ALU.min))
            vop(lambda e: e.scalar_tensor_tensor(yn[:, :W], B[:, :W], 2.0,
                                                 yn[:, :W], ALU.mult,
                                                 ALU.subtract))
            vop(lambda e: e.tensor_scalar(A[:, :W], B[:, :W], 0.5, 2.0,
                                          ALU.mult, ALU.min))
            vop(lambda e: e.tensor_tensor(B[:, :W], yn[:, :W], A[:, :W],
                                          ALU.subtract))
            vop(lambda e: e.tensor_tensor(r3(xq), r3(B), bc(scl), ALU.mult))
            return vop(lambda e: e.tensor_tensor(xq[:, :W], xq[:, :W],
                                                 SGN[:, :W], ALU.mult),
                       waits=[("s", sgc)])

        for ch in range(NCH):
            dma_raw = tmix[:, ch * 128:(ch + 1) * 128].rearrange("f p -> p f")
            p.dma("y", lambda e, o=tmc(ch), i=dma_raw: e.dma_start(
                out=o, in_=i, allow_slow_non_contiguous=True))
        dma("y", MPREV, mprev_d[:, :])
        dma("y", MSEL, msel_d[:, :])
        dma("y", IDN, ident_d[:, :])
        yfence0 = p.qfence("y")
        for ch in range(NCH):
            t, dd = tmc(ch), dvc(ch)
            vop(lambda e, t=t, dd=dd: e.tensor_scalar(
                dd[:, 0:3], t[:, 2:5], -1.0, 1.0, ALU.mult, ALU.add),
                waits=yfence0 if ch == 0 else ())
            vop(lambda e, t=t, dd=dd: e.tensor_scalar(
                dd[:, 3:4], t[:, 1:2], -1.0, None, ALU.mult))
        vop(lambda e: e.memset(ONESC, 1.0))
        vop(lambda e: e.memset(EPSC, EPS))
        vop(lambda e: e.memset(ONESR, 1.0))
        vop(lambda e: e.memset(JUNK, 0.0))
        v_const = vop(lambda e: e.memset(HAL, 0.0))

        tr_cnt, ev_cnt = [], []
        for tc in range(16):
            buf = L[:, (tc % 2) * 1024:(tc % 2) * 1024 + 1024]
            w = [("t", tr_cnt[tc - 2])] if tc >= 2 else []
            yc = dma("y", buf, x_in[tc * 128:(tc + 1) * 128, :], waits=w)
            tcn = 0
            for d in range(NCH):
                w2 = []
                if d == 0:
                    w2 = [yc, ("v", v_const)]
                    if tc >= 1:
                        w2.append(("s", ev_cnt[tc - 1]))
                tcn = top(lambda e, b=buf, d=d: e.transpose(
                    psE[:, d * 128:(d + 1) * 128] if d < 8 else None,
                    b[:, d * 128:(d + 1) * 128], IDN), waits=w2)
            tr_cnt.append(tcn)
            scn = 0
            for d in range(NCH):
                scn = sop(lambda e, d=d, tc=tc: e.copy(
                    R[:, d * TPC + tc * 128:d * TPC + tc * 128 + 128],
                    psE[:, d * 128:(d + 1) * 128]),
                    waits=[("t", tcn)] if d == 0 else ())
            ev_cnt.append(scn)
        s_xload = p.cnt["s"]

        state = {"psAD_free_v": 0, "psEF_free_v": 0, "psEF_free_s": s_xload,
                 "gus_free_t": 0, "wds_free_v": 0}

        prep_done = {}
        prep_lastv = [0]
        prep_lastg = [None]

        def prep_slice(wT, dch, off, W, store):
            yc = dma("y", wib[:, :W],
                     wT[dch * 128:(dch + 1) * 128, off:off + W],
                     waits=[("v", prep_lastv[0])])
            state["wib_free_y"] = yc
            vtrans(wrp[:, :W], wib[:, :W], waits=[yc])
            quant(wrp, W)
            vc = vtrans(wrp2[:, :W], qb["xq"][:, :W],
                        waits=[prep_lastg[0]])
            prep_lastv[0] = vc
            g = 0
            for m in range(W // 128):
                g = store((off + m * 128) // 128,
                          wrp2[:, m * 128:(m + 1) * 128], vc)
            prep_lastg[0] = g

        def emit_prep_gu_eslice(s_i):
            prep_lastv[0] = max(prep_lastv[0], p.cnt["v"])
            W = 256 if s_i == 5 else 512
            off = s_i * 512
            for widx, wT in ((0, wgT), (1, wuT)):
                for dch in range(NCH):
                    prep_slice(wT, dch, off, W,
                               lambda j, ap, vc, widx=widx, dch=dch:
                               dma("g", sc_gu[j, widx, dch], ap,
                                   waits=[("v", vc)]))
            for j in range(off // 128, (off + W) // 128):
                prep_done[("gu", j)] = p.qfence("g")

        def emit_prep_wd():
            prep_lastv[0] = max(prep_lastv[0], p.cnt["v"])
            for jch in range(NJ):
                for s_i in range(2):
                    prep_slice(wdT, jch, s_i * 512, 512,
                               lambda dm, ap, vc, jch=jch:
                               dma("g", sc_wd[jch, dm], ap, waits=[("v", vc)]))
                prep_done[("d", jch)] = p.qfence("g")

        def ln_stats_512(col_of, statrow):
            lastt = 0
            mmc = {}
            for ch in range(NCH):
                par = ch % 2
                a = col_of(ch)
                wv = [("s", state["psEF_free_s"])] if ch == 0 else []
                if ch >= 2:
                    wv.append(("t", mmc[ch - 2]))
                vc = vop(lambda e, a=a, par=par: e.tensor_tensor(
                    SQ[:, par * 512:par * 512 + 512], a, a, ALU.mult),
                    waits=wv)
                w = [("v", vc)]
                if ch == 0:
                    w += [("v", state["psEF_free_v"]),
                          ("s", state["psEF_free_s"])]
                top(lambda e, a=a, ch=ch: e.matmul(
                    psE[0:1, 0:512], ONESC, a,
                    start=(ch == 0), stop=(ch == NCH - 1)), waits=w)
                lastt = top(lambda e, par=par, ch=ch: e.matmul(
                    psE[0:1, 512:1024], ONESC, SQ[:, par * 512:par * 512 + 512],
                    start=(ch == 0), stop=(ch == NCH - 1)))
                mmc[ch] = lastt
            vop(lambda e: e.tensor_scalar(STW[0:1, 0:512], psE[0:1, 0:512],
                                          1.0 / D, None, ALU.mult),
                waits=[("t", lastt)])
            vop(lambda e: e.tensor_scalar(STW[0:1, 512:1024],
                                          psE[0:1, 512:1024],
                                          1.0 / D, None, ALU.mult))
            vop(lambda e: e.tensor_tensor(psE[0:1, 0:512], STW[0:1, 0:512],
                                          STW[0:1, 0:512], ALU.mult))
            vc = vop(lambda e: e.tensor_tensor(psF[0:1, 0:512],
                                               STW[0:1, 512:1024],
                                               psE[0:1, 0:512], ALU.subtract))
            scc = sop(lambda e: e.activation(STW[0:1, 512:1024],
                                             psF[0:1, 0:512],
                                             AF.Sqrt, bias=EPSC[0:1, 0:1]),
                      waits=[("v", vc)])
            vrecip(statrow[0:1, 0:512], STW[0:1, 512:1024],
                   waits=[("s", scc)])
            return vop(lambda e, sr=statrow: e.tensor_tensor(
                sr[0:1, 512:1024], STW[0:1, 0:512], sr[0:1, 0:512], ALU.mult))

        def ln_bcast(statrow, wv):
            top(lambda e, sr=statrow: e.matmul(
                psF[:, 0:512], ONESR, sr[0:1, 0:512], start=True, stop=True),
                waits=[("v", wv)])
            return top(lambda e, sr=statrow: e.matmul(
                psF[:, 512:1024], ONESR, sr[0:1, 512:1024],
                start=True, stop=True))

        def ffn_call(call):
            for tt in range(NTT):
                t0 = tt * TTW
                sv = ln_stats_512(
                    lambda ch: R[:, ch * TPC + t0:ch * TPC + t0 + TTW], STF)
                tb = ln_bcast(STF, sv)
                lastq = 0
                for ch in range(NCH):
                    vop(lambda e, ch=ch, t0=t0: e.tensor_tensor(
                        wib[:, 0:512], R[:, ch * TPC + t0:ch * TPC + t0 + TTW],
                        psF[:, 0:512], ALU.mult),
                        waits=[("t", tb), state.get("wib_free_y")]
                        if ch == 0 else ())
                    vop(lambda e: e.tensor_tensor(
                        wib[:, 0:512], wib[:, 0:512], psF[:, 512:1024],
                        ALU.subtract))
                    if ch == 0 and call == 0 and tt == 0:
                        dump("xln0", wib[:, 0:512], 512)
                    vtrans(wrp[:, 0:512], wib[:, 0:512])
                    if ch == 0 and call == 0 and tt == 0:
                        dump("wrp0", wrp[:, 0:512], 512)
                    quant(wrp, 512)
                    if ch == 0 and call == 0 and tt == 0:
                        dump("xqw0", qb["xq"][:, 0:512], 512)
                    lastq = vtrans(XQ[:, ch * TTW:(ch + 1) * TTW],
                                   qb["xq"][:, 0:512])
                    if ch == 0 and call == 0 and tt == 0:
                        dump("xq0", XQ[:, 0:TTW], 512)
                state["psEF_free_v"] = lastq
                state["psEF_free_s"] = 0
                hs_cnt, hv_cnt, gu_tc = {}, {}, {}
                for j in range(NJ):
                    if call == 0 and tt == 0 and j % 4 == 0:
                        emit_prep_gu_eslice(j // 4)
                        if j == 20:
                            emit_prep_gu_eslice(5)
                    par = j % 2
                    stg = GUS[:, par * 2048:par * 2048 + 2048]
                    w = [prep_done[("gu", j)]]
                    if j >= 2:
                        w.append(("t", gu_tc[j - 2]))
                    else:
                        w.append(("t", state["gus_free_t"]))
                    yc = dma("y", stg.rearrange("p (w d f) -> p w d f",
                                                w=2, d=NCH),
                             sc_gu[j].rearrange("w d p f -> p w d f"),
                             waits=w)
                    pg = psA if par == 0 else psC
                    pu = psB if par == 0 else psD
                    tc = 0
                    for d in range(NCH):
                        w2 = []
                        if d == 0:
                            w2 = [yc, ("v", lastq)]
                            if j >= 2:
                                w2.append(("v", hv_cnt[j - 2]))
                            else:
                                w2.append(("v", state["psAD_free_v"]))
                        top(lambda e, st=stg, d=d, pg=pg: e.matmul(
                            pg[:, 0:512], st[:, d * 128:(d + 1) * 128],
                            XQ[:, d * TTW:(d + 1) * TTW],
                            start=(d == 0), stop=(d == NCH - 1)), waits=w2)
                        tc = top(lambda e, st=stg, d=d, pu=pu: e.matmul(
                            pu[:, 0:512],
                            st[:, 1024 + d * 128:1024 + (d + 1) * 128],
                            XQ[:, d * TTW:(d + 1) * TTW],
                            start=(d == 0), stop=(d == NCH - 1)))
                    gu_tc[j] = tc
                    w3 = [("t", tc)]
                    if j >= 1:
                        w3.append(("v", hv_cnt[j - 1]))
                    hs = sop(lambda e, pg=pg: e.activation(
                        hb[:, 0:512], pg[:, 0:512], AF.Silu), waits=w3)
                    hs_cnt[j] = hs
                    vop(lambda e, pu=pu: e.tensor_tensor(
                        hb2[:, 0:512], hb[:, 0:512], pu[:, 0:512], ALU.mult),
                        waits=[("s", hs)])
                    if j == 0 and call == 0 and tt == 0:
                        dump("hb20", hb2[:, 0:512], 512)
                    vtrans(wrp[:, 0:512], hb2[:, 0:512])
                    quant(wrp, 512)
                    hv_cnt[j] = vtrans(HQ[:, j * TTW:(j + 1) * TTW],
                                       qb["xq"][:, 0:512])
                    if j == 0 and call == 0 and tt == 0:
                        dump("hq0", HQ[:, 0:TTW], 512)
                if call == 0 and tt == 0:
                    emit_prep_wd()

                def dn_ps(dm):
                    if dm < 4:
                        return (psA, psB, psC, psD)[dm], 0
                    if dm < 6:
                        return psE, (dm - 4) * 512
                    return psF, (dm - 6) * 512
                dn_tc = 0
                for j in range(NJ):
                    par = j % 2
                    stg = WDS[:, par * 1024:par * 1024 + 1024]
                    w = [prep_done[("d", j)], ("v", hv_cnt[j])]
                    if j >= 2:
                        w.append(("t", dn_tc))
                    else:
                        w.append(("v", state["wds_free_v"]))
                    yc = dma("y", stg.rearrange("p (d f) -> p d f", d=NCH),
                             sc_wd[j].rearrange("d p f -> p d f"), waits=w)
                    for dm in range(NCH):
                        ps, col = dn_ps(dm)
                        w2 = []
                        if dm == 0:
                            w2 = [yc]
                            if j == 0:
                                w2 += [("v", hv_cnt[NJ - 1]),
                                       ("s", hs_cnt[NJ - 1])]
                        dn_tc = top(lambda e, st=stg, dm=dm, ps=ps, col=col,
                                    j=j: e.matmul(
                            ps[:, col:col + 512],
                            st[:, dm * 128:(dm + 1) * 128],
                            HQ[:, j * TTW:(j + 1) * TTW],
                            start=(j == 0), stop=(j == NJ - 1)), waits=w2)
                ev = 0
                for dm in range(NCH):
                    ps, col = dn_ps(dm)
                    ev = vop(lambda e, ps=ps, col=col, dm=dm, t0=t0:
                             e.scalar_tensor_tensor(
                                 R[:, dm * TPC + t0:dm * TPC + t0 + TTW],
                                 ps[:, col:col + 512], 0.5,
                                 R[:, dm * TPC + t0:dm * TPC + t0 + TTW],
                                 ALU.mult, ALU.add),
                             waits=[("t", dn_tc)] if dm == 0 else ())
                state["psAD_free_v"] = ev
                state["psEF_free_v"] = ev
                state["psEF_free_s"] = 0
                state["gus_free_t"] = gu_tc[NJ - 1]

        def attn():
            for _c in range(NCH):
                dump(f"x1_{_c}", R[:, _c * TPC:_c * TPC + 512], 512)
            for ch in range(NCH):
                dma("g", sc_park[ch], R[:, ch * TPC:(ch + 1) * TPC],
                    waits=[("v", state["psAD_free_v"])] if ch == 0 else ())
            park_g = p.qfence("g")
            st2row = lambda t4: XQ[:, t4 * 1024:(t4 + 1) * 1024]
            prev_stat = None
            for t4 in range(NTT):
                t0 = t4 * TTW
                prevv = {}
                mmc = {}
                for ch in range(NCH):
                    bsl = (wib, wrp)[ch % 2]
                    w = list(park_g)
                    if ch >= 2:
                        w.append(("t", mmc[ch - 2]))
                    elif prev_stat is not None:
                        w.append(("t", prev_stat[0]))
                        w.append(("v", prev_stat[1]))
                    yc = dma("y", bsl[:, 0:512], sc_park[ch][:, t0:t0 + TTW],
                             waits=w)
                    par = ch % 2
                    w2 = [yc]
                    if ch >= 2:
                        w2.append(("t", mmc[ch - 2]))
                    if ch == 0:
                        w2 += [("v", state["psEF_free_v"]),
                               ("s", state["psEF_free_s"])]
                    vc = vop(lambda e, b=bsl, par=par: e.tensor_tensor(
                        SQ[:, par * 512:par * 512 + 512], b[:, 0:512],
                        b[:, 0:512], ALU.mult), waits=w2)
                    prevv[ch] = vc
                    top(lambda e, b=bsl, ch=ch: e.matmul(
                        psE[0:1, 0:512], ONESC, b[:, 0:512],
                        start=(ch == 0), stop=(ch == NCH - 1)),
                        waits=[("v", vc)])
                    tcl = top(lambda e, par=par, ch=ch: e.matmul(
                        psE[0:1, 512:1024], ONESC,
                        SQ[:, par * 512:par * 512 + 512],
                        start=(ch == 0), stop=(ch == NCH - 1)))
                    mmc[ch] = tcl
                prev_stat = (tcl, prevv[NCH - 1])
                vop(lambda e: e.tensor_scalar(STW[0:1, 0:512], psE[0:1, 0:512],
                                              1.0 / D, None, ALU.mult),
                    waits=[("t", tcl)])
                vop(lambda e: e.tensor_scalar(STW[0:1, 512:1024],
                                              psE[0:1, 512:1024],
                                              1.0 / D, None, ALU.mult))
                vop(lambda e: e.tensor_tensor(psE[0:1, 0:512], STW[0:1, 0:512],
                                              STW[0:1, 0:512], ALU.mult))
                vc = vop(lambda e: e.tensor_tensor(
                    psF[0:1, 0:512], STW[0:1, 512:1024], psE[0:1, 0:512],
                    ALU.subtract))
                scc = sop(lambda e: e.activation(STW[0:1, 512:1024],
                                                 psF[0:1, 0:512], AF.Sqrt,
                                                 bias=EPSC[0:1, 0:1]), waits=[("v", vc)])
                vrecip(st2row(t4)[0:1, 0:512], STW[0:1, 512:1024],
                       waits=[("s", scc)])
                state["psEF_free_v"] = vop(lambda e, t4=t4: e.tensor_tensor(
                    st2row(t4)[0:1, 512:1024], STW[0:1, 0:512],
                    st2row(t4)[0:1, 0:512], ALU.mult))
                state["psEF_free_s"] = 0
                if t4 == 0:
                    dump("st20", st2row(0)[:, 0:1024], 1024)
            ap_cnt = {}
            halo_v = 0
            prev_ap = None
            for t4 in (3, 0, 1, 2):
                t0 = t4 * TTW
                tb = ln_bcast(st2row(t4), state["psEF_free_v"])
                for ch in range(NCH):
                    bsl = (wib, wrp)[ch % 2]
                    w = []
                    if (t4, ch - 2) in ap_cnt:
                        w.append(("v", ap_cnt[(t4, ch - 2)]))
                    elif prev_ap is not None:
                        w.append(("v", ap_cnt[(prev_ap, ch + 6)]))
                    else:
                        w.append(("t", prev_stat[0]))
                        w.append(("v", prev_stat[1]))
                    yc = dma("y", bsl[:, 0:512], sc_park[ch][:, t0:t0 + TTW],
                             waits=w)
                    vop(lambda e, b=bsl: e.tensor_tensor(
                        hb[:, 0:512], b[:, 0:512], psF[:, 0:512], ALU.mult),
                        waits=[yc, ("t", tb)])
                    ap_cnt[(t4, ch)] = vop(
                        lambda e, ch=ch, t0=t0: e.tensor_tensor(
                            R[:, ch * TPC + t0:ch * TPC + t0 + TTW],
                            hb[:, 0:512], psF[:, 512:1024], ALU.subtract))
                    if t4 == 0 and ch == 0:
                        dump("xln2_0", R[:, 0:512], 512)
                state["psEF_free_v"] = ap_cnt[(t4, NCH - 1)]
                prev_ap = t4
                if t4 == 3:
                    hcp = 0
                    for ch in range(NCH):
                        hcp = vop(lambda e, ch=ch: e.tensor_copy(
                            W8[:, ch:ch + 1],
                            R[:, ch * TPC + TPC - 1:ch * TPC + TPC]),
                            waits=[("v", ap_cnt[(3, ch)])] if ch == 0 else ())
                    gtok = dma("g", cc_h_in[:, :], W8, waits=[("v", hcp)])
                    ccc = gop(lambda e: e.collective_compute(
                        "AllGather", ALU.bypass, ins=[cc_h_in[:, :]],
                        outs=[cc_h_out[:, :, :]],
                        replica_groups=[list(range(NCORE))]),
                        waits=[gtok])
                    g1 = dma("g", HG.rearrange("p (c f) -> p c f", c=8),
                             cc_h_out[:, :, :].rearrange("c p f -> p c f"),
                             waits=[("c", ccc)])
                    vc = vop(lambda e: e.tensor_tensor(
                        HG.rearrange("p (c f) -> p c f", c=8),
                        HG.rearrange("p (c f) -> p c f", c=8),
                        MPREV.rearrange("p (c one) -> p c one", one=1)
                        .broadcast_to([128, 8, 8]), ALU.mult),
                        waits=[g1])
                    halo_v = vspecial(lambda e: e.reduce_sum(
                        HAL.rearrange("p (ch one) -> p ch one", one=1),
                        HG.rearrange("p (c ch) -> p ch c", c=8), axis=AXX))
            ap_last = ap_cnt[(2, NCH - 1)]

            def emit_mix(kind, wextra=()):
                lastv = 0
                for ch in range(NCH):
                    mx = mixslot(ch)
                    tm_ap = tmc(ch)[:, 2 + kind:3 + kind]
                    om_ap = dvc(ch)[:, kind:kind + 1]
                    base = ch * TPC
                    w = list(wextra) if ch == 0 else []
                    vop(lambda e, ch=ch, om=om_ap, mx=mx: e.tensor_scalar(
                        mx[:, 0:1], HAL[:, ch:ch + 1], om, None, ALU.mult),
                        waits=w)
                    vop(lambda e, tm=tm_ap, mx=mx, base=base:
                        e.scalar_tensor_tensor(
                            mx[:, 0:1], R[:, base:base + 1], tm, mx[:, 0:1],
                            ALU.mult, ALU.add))
                    vop(lambda e, om=om_ap, mx=mx, base=base: e.tensor_scalar(
                        mx[:, 1:2048], R[:, base:base + 2047], om, None,
                        ALU.mult))
                    lastv = vop(lambda e, tm=tm_ap, mx=mx, base=base:
                                e.scalar_tensor_tensor(
                                    mx[:, 1:2048], R[:, base + 1:base + 2048],
                                    tm, mx[:, 1:2048], ALU.mult, ALU.add))
                return lastv

            aw_lt = {0: state["gus_free_t"], 1: state["gus_free_t"]}

            def emit_proj(wT, mix_v, evac_fn, wait_extra=()):
                for ech in range(NCH):
                    par = ech % 2
                    stg = GUS[:, par * 1024:par * 1024 + 1024]
                    ycs = []
                    for d in range(NCH):
                        ycs.append(dma("y", stg[:, d * 128:(d + 1) * 128],
                                   wT[d * 128:(d + 1) * 128,
                                      ech * 128:(ech + 1) * 128],
                                   waits=[("t", aw_lt.get(par, 0))]
                                   if d == 0 else ()))
                    ps4 = ([(psE, 0), (psE, 512), (psF, 0), (psF, 512)]
                           if par == 0 else
                           [(psA, 0), (psB, 0), (psC, 0), (psD, 0)])
                    tcl = 0
                    for d in range(NCH):
                        w = []
                        if d == 0:
                            w = ycs + [("v", mix_v),
                                 ("v", state["psEF_free_v" if par == 0
                                             else "psAD_free_v"])]
                            if par == 0 and state["psEF_free_s"]:
                                w.append(("s", state["psEF_free_s"]))
                            w += list(wait_extra)
                        for q4 in range(4):
                            ps, col = ps4[q4]
                            tcl = top(lambda e, stg=stg, d=d, ps=ps, col=col,
                                      q4=q4: e.matmul(
                                ps[:, col:col + 512],
                                stg[:, d * 128:(d + 1) * 128],
                                mixslot(d)[:, q4 * 512:q4 * 512 + 512],
                                start=(d == 0), stop=(d == NCH - 1)),
                                waits=w if q4 == 0 else ())
                    aw_lt[par] = tcl
                    fc = evac_fn(ech, ps4, tcl)
                    if par == 0:
                        state["psEF_free_v"] = fc
                        state["psEF_free_s"] = 0
                    else:
                        state["psAD_free_v"] = fc

            kmix_v = emit_mix(0, wextra=[("v", halo_v), ("v", ap_last)])
            dump("key0", mixslot(0)[:, 0:512], 512)
            dump("key0b", mixslot(0)[:, 512:1024], 512)
            kspill_g = {}

            def k_evac(ech, ps4, tcl):
                dst = (b1, b2)[ech % 2]
                vc = 0
                for q4 in range(4):
                    ps, col = ps4[q4]
                    vc = vop(lambda e, ps=ps, col=col, dst=dst, q4=q4:
                             e.tensor_copy(dst[:, q4 * 512:q4 * 512 + 512],
                                           ps[:, col:col + 512]),
                             waits=([("t", tcl)] +
                                    ([kspill_g.get(ech - 2)]
                                     if ech >= 2 else [])) if q4 == 0 else ())
                if ech == 0:
                    dump("k0", dst[:, 0:512], 512)
                kspill_g[ech] = dma("g", sc_k[ech], dst, waits=[("v", vc)])
                return vc
            emit_proj(wkT, kmix_v, k_evac)

            vmix_v = emit_mix(1, wextra=[("t", aw_lt[0]), ("t", aw_lt[1])])
            espill_g = {}
            last_fc = [0]

            def v_evac(ech, ps4, tcl):
                ch = ech
                for q4 in range(4):
                    ps, col = ps4[q4]
                    vop(lambda e, ps=ps, col=col, q4=q4: e.tensor_copy(
                        b4[:, q4 * 512:q4 * 512 + 512], ps[:, col:col + 512]),
                        waits=([("t", tcl), kspill_g[NCH - 1],
                                ("v", last_fc[0])]
                               if q4 == 0 else ()))
                yk = dma("g", b1, sc_k[ch][:, :],
                         waits=[espill_g.get(ch - 1), kspill_g[NCH - 1],
                                ("v", last_fc[0])])
                u_ap = tmc(ch)[:, 0:1]
                w_ap = tmc(ch)[:, 1:2]
                nw_ap = dvc(ch)[:, 3:4]
                vop(lambda e, u=u_ap, w=w_ap: e.tensor_scalar(
                    b2, b1, u, w, ALU.add, ALU.max), waits=[yk])
                vspecial(lambda e, ch=ch: e.reduce_sum(
                    SQS[:, ch * 2:ch * 2 + 1]
                    .rearrange("p (b one) -> p b one", one=1),
                    b2[:, 0:1024].rearrange("p (b k) -> p b k", k=1024),
                    axis=AXX))
                vspecial(lambda e, ch=ch: e.reduce_sum(
                    SQS[:, ch * 2 + 1:ch * 2 + 2]
                    .rearrange("p (b one) -> p b one", one=1),
                    b2[:, 1024:2048].rearrange("p (b k) -> p b k", k=1024),
                    axis=AXX))
                vc = vop(lambda e: e.tensor_tensor(b3, b1, b2, ALU.subtract))
                s1c = sop(lambda e, nw=nw_ap: e.activation(
                    b1, b2, AF.Exp, bias=nw, scale=-1.0), waits=[("v", vc)])
                s2c = sop(lambda e, u=u_ap: e.activation(
                    b2, b3, AF.Exp, bias=u, scale=1.0))
                ev2 = vop(lambda e: e.tensor_tensor(b3, b2, b4, ALU.mult),
                          waits=[("s", s2c)])
                if ch == 0:
                    dump("e1_0", b1[:, 0:512], 512)
                    dump("e2v_0", b3[:, 0:512], 512)
                g1 = dma("g", sc_e[ch, 0], b1, waits=[("s", s1c)])
                g2 = dma("g", sc_e[ch, 1], b2, waits=[("s", s2c)])
                g3 = dma("g", sc_e[ch, 2], b3, waits=[("v", ev2)])
                espill_g[ch] = g3
                vscan(b4[:, 0:1024], b1[:, 0:1024], b3[:, 0:1024], 0.0,
                      waits=[("s", s1c)])
                vscan(b4[:, 1024:2048], b1[:, 1024:2048], b3[:, 1024:2048],
                      0.0)
                vscan(b3[:, 0:1024], b1[:, 0:1024], b2[:, 0:1024], 0.0,
                      waits=[g3])
                vc = vscan(b3[:, 1024:2048], b1[:, 1024:2048],
                           b2[:, 1024:2048], 0.0)
                vop(lambda e, ch=ch: e.tensor_tensor(
                    W8[:, 0:1], SQS[:, ch * 2:ch * 2 + 1],
                    SQS[:, ch * 2 + 1:ch * 2 + 2], ALU.add))
                vc2 = vop(lambda e, ch=ch, w=w_ap: e.scalar_tensor_tensor(
                    W8[:, 1:2], w, -2048.0, W8[:, 0:1], ALU.mult,
                    ALU.subtract))
                sA = sop(lambda e, ch=ch: e.activation(
                    AFULL[:, ch:ch + 1], W8[:, 1:2], AF.Exp),
                    waits=[("v", vc2)])
                vc2 = vop(lambda e, ch=ch, w=w_ap: e.scalar_tensor_tensor(
                    W8[:, 2:3], w, -1024.0, SQS[:, ch * 2 + 1:ch * 2 + 2],
                    ALU.mult, ALU.subtract))
                sP = sop(lambda e, ch=ch: e.activation(
                    P1E[:, ch:ch + 1], W8[:, 2:3], AF.Exp),
                    waits=[("v", vc2)])
                vop(lambda e, ch=ch: e.scalar_tensor_tensor(
                    ENDS[:, ch * 3 + 1:ch * 3 + 2], b4[:, 1023:1024],
                    P1E[:, ch:ch + 1], b4[:, 2047:2048], ALU.mult, ALU.add),
                    waits=[("s", sP)])
                vop(lambda e, ch=ch: e.scalar_tensor_tensor(
                    ENDS[:, ch * 3 + 2:ch * 3 + 3], b3[:, 1023:1024],
                    P1E[:, ch:ch + 1], b3[:, 2047:2048], ALU.mult, ALU.add))
                fc = vop(lambda e, ch=ch: e.tensor_copy(
                    ENDS[:, ch * 3:ch * 3 + 1], AFULL[:, ch:ch + 1]),
                    waits=[("s", sA)])
                last_fc[0] = fc
                return fc
            emit_proj(wvT, vmix_v, v_evac)
            ends_v = p.cnt["v"]

            sig_c = 0
            for ch in range(NCH):
                mx = mixslot(ch)
                tm_ap = tmc(ch)[:, 4:5]
                om_ap = dvc(ch)[:, 2:3]
                base = ch * TPC
                w = ([("t", aw_lt[0]), ("t", aw_lt[1]),
                      ("v", last_fc[0])] if ch == 0
                     else [("s", sig_c)])
                vop(lambda e, ch=ch, om=om_ap: e.tensor_scalar(
                    b1[:, 0:1], HAL[:, ch:ch + 1], om, None, ALU.mult),
                    waits=w)
                vop(lambda e, tm=tm_ap, base=base: e.scalar_tensor_tensor(
                    b1[:, 0:1], R[:, base:base + 1], tm, b1[:, 0:1],
                    ALU.mult, ALU.add))
                vop(lambda e, om=om_ap, base=base: e.tensor_scalar(
                    b1[:, 1:2048], R[:, base:base + 2047], om, None,
                    ALU.mult))
                rmx = vop(lambda e, tm=tm_ap, base=base:
                          e.scalar_tensor_tensor(
                              b1[:, 1:2048], R[:, base + 1:base + 2048],
                              tm, b1[:, 1:2048], ALU.mult, ALU.add))
                sig_c = sop(lambda e, mx=mx: e.activation(
                    mx, b1[:, 0:2048], AF.Sigmoid), waits=[("v", rmx)])
                if ch == 0:
                    p.bare_wait("v", [("s", sig_c)])
                    dump("rec0", mixslot(0)[:, 0:512], 512)

            def r_evac(ech, ps4, tcl):
                vc = 0
                for q4 in range(4):
                    ps, col = ps4[q4]
                    vc = vop(lambda e, ps=ps, col=col, ech=ech, q4=q4:
                             e.tensor_copy(
                                 R[:, ech * TPC + q4 * 512:
                                   ech * TPC + q4 * 512 + 512],
                                 ps[:, col:col + 512]),
                             waits=[("t", tcl)] if q4 == 0 else ())
                if ech == 0:
                    dump("rev0", R[:, 0:512], 512)
                return vc
            emit_proj(wrT, p.cnt["v"], r_evac, wait_extra=[("s", sig_c)])

            getok = dma("g", cc_e_in[:, :], ENDS, waits=[("v", ends_v)])
            ccc2 = gop(lambda e: e.collective_compute(
                "AllGather", ALU.bypass, ins=[cc_e_in[:, :]],
                outs=[cc_e_out[:, :, :]],
                replica_groups=[list(range(NCORE))]), waits=[getok])
            ge = dma("g", EG.rearrange("p (c f) -> p c f", c=8),
                     cc_e_out[:, :, :].rearrange("c p f -> p c f"),
                     waits=[("c", ccc2)])
            eg4 = lambda: EG.rearrange("p (c ch three) -> p c ch three",
                                       c=8, three=3)
            vop(lambda e: e.memset(AINW[:, 0:8], 0.0), waits=[ge])
            vop(lambda e: e.memset(BINW[:, 0:8], 0.0))
            for c in range(1, 8):
                vop(lambda e, c=c: e.tensor_tensor(
                    AINW[:, c * 8:(c + 1) * 8], eg4()[:, c - 1, :, 0],
                    AINW[:, (c - 1) * 8:c * 8], ALU.mult))
                vop(lambda e, c=c: e.tensor_tensor(
                    AINW[:, c * 8:(c + 1) * 8], AINW[:, c * 8:(c + 1) * 8],
                    eg4()[:, c - 1, :, 1], ALU.add))
                vop(lambda e, c=c: e.tensor_tensor(
                    BINW[:, c * 8:(c + 1) * 8], eg4()[:, c - 1, :, 0],
                    BINW[:, (c - 1) * 8:c * 8], ALU.mult))
                vop(lambda e, c=c: e.tensor_tensor(
                    BINW[:, c * 8:(c + 1) * 8], BINW[:, c * 8:(c + 1) * 8],
                    eg4()[:, c - 1, :, 2], ALU.add))
            for src, dst in ((AINW, AIN), (BINW, BIN)):
                vop(lambda e, s=src: e.tensor_tensor(
                    HG.rearrange("p (c f) -> p c f", c=8),
                    s.rearrange("p (c f) -> p c f", c=8),
                    MSEL.rearrange("p (c one) -> p c one", one=1)
                    .broadcast_to([128, 8, 8]), ALU.mult))
                vspecial(lambda e, d=dst: e.reduce_sum(
                    d.rearrange("p (ch one) -> p ch one", one=1),
                    HG.rearrange("p (c ch) -> p ch c", c=8), axis=AXX))
            sel_v = vjunk()

            p2last = {}
            for ch in range(NCH):
                w = [("v", sel_v)]
                if ch >= 1:
                    w.append(("v", p2last[ch - 1]))
                y1 = dma("g", b1, sc_e[ch, 0], waits=w)
                y3 = dma("g", b3, sc_e[ch, 2])
                vscan(b4[:, 0:1024], b1[:, 0:1024], b3[:, 0:1024],
                      AIN[:, ch:ch + 1], waits=[y1, y3])
                av = vscan(b4[:, 1024:2048], b1[:, 1024:2048],
                           b3[:, 1024:2048], b4[:, 1023:1024])
                y2 = dma("g", b3, sc_e[ch, 1], waits=[("v", av)])
                vscan(b2[:, 0:1024], b1[:, 0:1024], b3[:, 0:1024],
                      BIN[:, ch:ch + 1], waits=[y2])
                vscan(b2[:, 1024:2048], b1[:, 1024:2048], b3[:, 1024:2048],
                      b2[:, 1023:1024])
                if ch == 0:
                    dump("a0", b4[:, 0:512], 512)
                    dump("b0", b2[:, 0:512], 512)
                vrecip(b1, b2)
                vop(lambda e: e.tensor_tensor(b3, b4, b1, ALU.mult))
                if ch == 0:
                    dump("wkv0", b3[:, 0:512], 512)
                    dump("r0", R[:, 0:512], 512)
                vop(lambda e, ch=ch: e.tensor_tensor(
                    b4, b3, R[:, ch * TPC:(ch + 1) * TPC], ALU.mult))
                p2last[ch] = vop(lambda e, ch=ch: e.tensor_copy(
                    R[:, ch * TPC:(ch + 1) * TPC], b4))

            x1_y = {}

            def o_evac(ech, ps4, tcl):
                yx = dma("g", b1, sc_park[ech][:, :],
                         waits=[("v", x1_y.get(ech - 1, p2last[NCH - 1]))])
                bo_ap = tmc(ech)[:, 5:6]
                vc = 0
                for q4 in range(4):
                    ps, col = ps4[q4]
                    vop(lambda e, ps=ps, col=col, bo=bo_ap: e.tensor_scalar(
                        hb[:, 0:512], ps[:, col:col + 512], bo, None, ALU.add),
                        waits=[("t", tcl), yx] if q4 == 0 else ())
                    vc = vop(lambda e, ech=ech, q4=q4: e.tensor_tensor(
                        mixslot(ech)[:, q4 * 512:q4 * 512 + 512], hb[:, 0:512],
                        b1[:, q4 * 512:q4 * 512 + 512], ALU.add))
                x1_y[ech] = vc
                return vc

            def proj_o():
                for ech in range(NCH):
                    par = ech % 2
                    stg = GUS[:, par * 1024:par * 1024 + 1024]
                    ycs = []
                    for d in range(NCH):
                        ycs.append(dma("y", stg[:, d * 128:(d + 1) * 128],
                                   woT[d * 128:(d + 1) * 128,
                                       ech * 128:(ech + 1) * 128],
                                   waits=[("t", aw_lt.get(par, 0))]
                                   if d == 0 else ()))
                    ps4 = ([(psE, 0), (psE, 512), (psF, 0), (psF, 512)]
                           if par == 0 else
                           [(psA, 0), (psB, 0), (psC, 0), (psD, 0)])
                    tcl = 0
                    for d in range(NCH):
                        w = []
                        if d == 0:
                            w = ycs + [("v", p2last[NCH - 1]),
                                 ("v", state["psEF_free_v" if par == 0
                                             else "psAD_free_v"])]
                        for q4 in range(4):
                            ps, col = ps4[q4]
                            tcl = top(lambda e, stg=stg, d=d, ps=ps, col=col,
                                      q4=q4: e.matmul(
                                ps[:, col:col + 512],
                                stg[:, d * 128:(d + 1) * 128],
                                R[:, d * TPC + q4 * 512:
                                  d * TPC + q4 * 512 + 512],
                                start=(d == 0), stop=(d == NCH - 1)),
                                waits=w if q4 == 0 else ())
                    aw_lt[par] = tcl
                    fc = o_evac(ech, ps4, tcl)
                    if par == 0:
                        state["psEF_free_v"] = fc
                    else:
                        state["psAD_free_v"] = fc
            proj_o()
            cp = 0
            for ch in range(NCH):
                cp = vop(lambda e, ch=ch: e.tensor_copy(
                    R[:, ch * TPC:(ch + 1) * TPC], mixslot(ch)),
                    waits=[("t", aw_lt[0]), ("t", aw_lt[1]),
                           ("v", x1_y[NCH - 1])] if ch == 0 else ())
            state["psAD_free_v"] = cp
            state["psEF_free_v"] = cp
            state["psEF_free_s"] = 0
            state["gus_free_t"] = max(aw_lt.values())
            state["wds_free_v"] = cp

        def emit_out():
            evs = {}
            for tc in range(16):
                par = tc % 2
                ps = psE if par == 0 else psF
                tcl = 0
                for d in range(NCH):
                    w = []
                    if d == 0:
                        w = [("v", state["psAD_free_v"]),
                             ("s", evs.get(tc - 2, 0)),
                             ("s", state["psEF_free_s"])]
                    tcl = top(lambda e, d=d, tc=tc, ps=ps: e.transpose(
                        ps[:, d * 128:(d + 1) * 128],
                        R[:, d * TPC + tc * 128:d * TPC + tc * 128 + 128],
                        IDN), waits=w)
                scn = 0
                dst = (b1, b2)[par]
                for d in range(NCH):
                    scn = sop(lambda e, d=d, ps=ps, dst=dst: e.copy(
                        dst[:, d * 128:(d + 1) * 128],
                        ps[:, d * 128:(d + 1) * 128]),
                        waits=([("t", tcl)] +
                               ([evs.get(("g", tc - 2))]
                                if tc >= 2 else [])) if d == 0 else ())
                evs[tc] = scn
                evs[("g", tc)] = dma("g", out_d[tc * 128:(tc + 1) * 128, :],
                                     dst[:, 0:1024], waits=[("s", scn)])
            p.bare_wait("g", p.qfence("g"))

        if stage != "copy":
            ffn_call(0)
        if stage in ("ffn1attn", "full"):
            attn()
        if stage == "full":
            ffn_call(1)
        emit_out()

        def mk(engname, engine):
            for w, fn, semn, inc in p.stream[engname]:
                for en, c in w:
                    engine.wait_ge(sems[en], c)
                if fn is not None:
                    fn(engine).then_inc(sems[semn], inc)

        @block.vector
        def _(vector):
            mk("v", vector)

        @block.scalar
        def _(scalar):
            mk("s", scalar)

        @block.tensor
        def _(tensor):
            mk("t", tensor)

        @block.sync
        def _(sync):
            mk("y", sync)

        @block.gpsimd
        def _(gp):
            mk("g", gp)

    return nc


def kernel(L0_attn_x, L0_attn_u, L0_attn_w, L0_attn_tmk, L0_attn_tmv,
           L0_attn_tmr, L0_attn_Wk, L0_attn_Wv, L0_attn_Wr, L0_attn_Wo,
           L0_attn_bo, L0_ffn_w_gate, L0_ffn_w_up, L0_ffn_w_down,
           _stage="full", _dbg=()):
    f32 = np.float32
    x = np.ascontiguousarray(np.asarray(L0_attn_x, dtype=f32).reshape(16384, D))
    cT = lambda a: np.ascontiguousarray(np.asarray(a, dtype=f32).T)
    wgT = cT(L0_ffn_w_gate)
    wuT = cT(L0_ffn_w_up)
    wdT = cT(L0_ffn_w_down)
    wkT, wvT, wrT, woT = (cT(L0_attn_Wk), cT(L0_attn_Wv),
                          cT(L0_attn_Wr), cT(L0_attn_Wo))
    tmixv = np.ascontiguousarray(np.stack([
        np.asarray(a, dtype=f32) for a in
        (L0_attn_u, L0_attn_w, L0_attn_tmk, L0_attn_tmv, L0_attn_tmr,
         L0_attn_bo)]))
    ident = np.eye(128, dtype=f32)
    nc = build(stage=_stage, dbg=_dbg)
    in_maps = []
    for c in range(NCORE):
        mprev = np.zeros((128, 8), dtype=f32)
        if c > 0:
            mprev[:, c - 1] = 1.0
        msel = np.zeros((128, 8), dtype=f32)
        msel[:, c] = 1.0
        in_maps.append({
            "x": np.ascontiguousarray(x[c * TPC:(c + 1) * TPC]),
            "wgT": wgT, "wuT": wuT, "wdT": wdT,
            "wkT": wkT, "wvT": wvT, "wrT": wrT, "woT": woT,
            "tmix": tmixv, "mprev": mprev, "msel": msel, "ident": ident,
        })
    res = run_bass_kernel_spmd(nc, in_maps, core_ids=list(range(NCORE)))
    outp = np.concatenate([res.results[c]["out"] for c in range(NCORE)],
                          axis=0)
    out = outp.reshape(1, 16384, D).astype(np.float32)
    if _dbg:
        return out, [res.results[c].get("dbg") for c in range(NCORE)]
    return out
